# version 02 (final)
# baseline (speedup 1.0000x reference)
import math
import numpy as np
import ml_dtypes
import concourse.bass as bass
import concourse.mybir as mybir
from concourse.bass_utils import run_bass_kernel_spmd

F32 = mybir.dt.float32
BF16 = mybir.dt.bfloat16
I32 = mybir.dt.int32
AF = mybir.ActivationFunctionType
ALU = mybir.AluOpType
AX = mybir.AxisListType

NCORE = 8
T = 2048
D = 1024
MT = T // 128
KD = D // 128

BF = np.dtype(ml_dtypes.bfloat16)


class P:

    ENGINES = ("sync", "scalar", "vector", "tensor", "gpsimd")

    def __init__(self, name):
        self.name = name
        self.nc = bass.Bass("TRN2", target_bir_lowering=False)
        self.items = []
        self.prelude = []
        self._ctx = []
        self._uid = 0
        self._kmap = {}
        self._ktile = None

    def konst(self, val):
        if self._ktile is None:
            self._ktile = self.sb((128, 16), F32, name=f"{self.name}_konsts")
        if val not in self._kmap:
            idx = len(self._kmap)
            assert idx < 16
            ap = self._ktile[:, idx:idx + 1]
            self.prelude.append(
                ("vector", lambda e, a=ap, v=val: e.memset(a, v), 1))
            self._kmap[val] = ap
        return self._kmap[val]

    def uid(self):
        self._uid += 1
        return f"{self.name}_{self._uid}"

    def dram_in(self, name, shape, dtype=F32):
        return self.nc.dram_tensor(name, list(shape), dtype, kind="ExternalInput").ap()

    def dram_out(self, name, shape, dtype=F32):
        return self.nc.dram_tensor(name, list(shape), dtype, kind="ExternalOutput").ap()

    def dram_tmp(self, name, shape, dtype=F32):
        return self.nc.dram_tensor(name, list(shape), dtype, kind="Internal").ap()

    def _enter(self, cm):
        v = cm.__enter__()
        self._ctx.append(cm)
        return v

    def sb(self, shape, dtype=F32, name=None):
        t = self._enter(self.nc.sbuf_tensor(name or self.uid(), list(shape), dtype))
        return t.ap()

    def ps(self, shape, dtype=F32, name=None):
        t = self._enter(self.nc.psum_tensor(name or self.uid(), list(shape), dtype))
        return t.ap()

    def op(self, eng, fn, inc=1):
        self.items.append((eng, fn, inc))

    def dma(self, dst, src, eng="sync"):
        self.op(eng, lambda e, d=dst, s=src: e.dma_start(out=d, in_=s), 16)

    def mm(self, out, lhsT, rhs, start=True, stop=True):
        self.op("tensor", lambda e, o=out, l=lhsT, r=rhs, s=start, st=stop:
                e.matmul(o, l, r, start=s, stop=st))

    def trans(self, out_ps, in_sb, ident):
        self.op("tensor", lambda e, o=out_ps, i=in_sb, d=ident:
                e.transpose(o, i, d))

    def act(self, out, in_, func, bias=0.0, scale=1.0, accum_out=None):
        def f(e, o=out, i=in_, fu=func, b=bias, s=scale, a=accum_out):
            kw = {}
            if a is not None:
                kw["accum_out"] = a
            return e.activation(o, i, fu, bias=b, scale=s, **kw)
        self.op("scalar", f)

    def copy(self, out, in_):
        self.op("scalar", lambda e, o=out, i=in_: e.copy(o, i))

    def vcopy(self, out, in_):
        self.op("vector", lambda e, o=out, i=in_: e.tensor_copy(out=o, in_=i))

    def tt(self, out, a, b, alu):
        self.op("vector", lambda e, o=out, x=a, y=b, al=alu: e.tensor_tensor(o, x, y, al))

    def ts(self, out, in_, scalar, alu):
        self.op("vector", lambda e, o=out, i=in_, s=scalar, al=alu:
                e.tensor_single_scalar(o, i, s, al))

    def ts2(self, out, in_, s1, op0, s2, op1):
        self.op("vector", lambda e, o=out, i=in_, a=s1, b=s2, o0=op0, o1=op1:
                e.tensor_scalar(o, i, a, b, o0, o1))

    def stt(self, out, in0, scalar, in1, op0, op1):
        self.op("vector", lambda e, o=out, a=in0, s=scalar, b=in1, o0=op0, o1=op1:
                e.scalar_tensor_tensor(o, a, s, b, o0, o1))

    def red(self, out, in_, alu, negate=False):
        self.op("vector", lambda e, o=out, i=in_, al=alu, n=negate:
                e.tensor_reduce(o, i, AX.X, al, negate=n))

    def recip(self, out, in_):
        self.op("vector", lambda e, o=out, i=in_: e.reciprocal(o, i))

    def scan(self, out, a, b, initial):
        self.op("vector", lambda e, o=out, x=a, y=b, i=initial:
                e.tensor_tensor_scan(o, x, y, i, ALU.mult, ALU.add))

    def memset(self, ap, val=0.0):
        self.op("vector", lambda e, a=ap, v=val: e.memset(a, v))

    def build(self):
        nc = self.nc
        sem = self._enter(nc.semaphore(f"{self.name}_chain"))
        block = self._enter(nc.Block())
        items = self.prelude + self.items
        waits = []
        c = 0
        for eng, fn, inc in items:
            waits.append(c)
            c += inc
        total = c

        def mk(engname):
            idxs = [i for i, (e, _, _) in enumerate(items) if e == engname]

            def emit(engobj):
                for i in idxs:
                    if waits[i] > 0:
                        engobj.wait_ge(sem, waits[i])
                    ins = items[i][1](engobj)
                    ins.then_inc(sem, items[i][2])
                if engname == "sync":
                    engobj.wait_ge(sem, total)
            return emit

        for en in self.ENGINES:
            if en == "sync" or any(e == en for e, _, _ in items):
                getattr(block, en)(mk(en))
        for cm in reversed(self._ctx):
            cm.__exit__(None, None, None)
        self._ctx = []
        return nc


def emit_ln(p, x, out, scr, stat, DD, eps=1e-5):
    mean = stat[:, 0:1]
    ssq = stat[:, 1:2]
    lnv = stat[:, 2:3]
    rstd = stat[:, 3:4]
    p.red(mean, x, ALU.add)
    p.op("scalar", lambda e, o=mean, i=mean: e.mul(o, i, 1.0 / DD))
    p.ts(out, x, mean, ALU.subtract)
    p.act(scr, out, AF.Square, accum_out=ssq)
    p.act(lnv, ssq, AF.Ln, scale=1.0 / DD, bias=p.konst(eps))
    p.act(rstd, lnv, AF.Exp, scale=-0.5)
    p.ts(out, out, rstd, ALU.mult)


def emit_rms(p, x, out, scr, stat, DD, eps):
    ssq = stat[:, 1:2]
    lnv = stat[:, 2:3]
    rstd = stat[:, 3:4]
    p.act(scr, x, AF.Square, accum_out=ssq)
    p.act(lnv, ssq, AF.Ln, scale=1.0 / DD, bias=p.konst(eps))
    p.act(rstd, lnv, AF.Exp, scale=-0.5)
    p.ts(out, x, rstd, ALU.mult)


def emit_cast_T(p, src_f32_or_bf, dst_tiles, ident, psT, bfscr=None, nk=None):
    src = src_f32_or_bf
    if bfscr is not None:
        p.vcopy(bfscr, src)
        src = bfscr
    for k in range(nk):
        p.trans(psT, src[:, k * 128:(k + 1) * 128], ident)
        p.vcopy(dst_tiles[k], psT)


def softplus_emit(p, p_in, out, t1, t2):
    p.op("vector", lambda e, o=t1, i=p_in: e.tensor_relu(o, i))
    p.act(t2, p_in, AF.Abs)
    p.act(t2, t2, AF.Exp, scale=-1.0)
    p.act(t2, t2, AF.Ln, bias=1.0)
    p.tt(out, t1, t2, ALU.add)


def _rope_tables_inner():
    n, HD, WIN = 2048, 64, 256
    half = np.arange(0, HD, 2, dtype=np.float64)
    inv_freq = 1.0 / 10000.0 ** (half / HD)
    t = np.arange(n, dtype=np.float64)
    freqs = t[:, None] * inv_freq[None, :]
    freqs = np.concatenate([freqs, freqs], -1)
    base_scale = (half + 0.4 * HD) / (1.4 * HD)
    power = (t - n // 2) / (WIN // 2)
    scale = base_scale[None, :] ** power[:, None]
    scale = np.concatenate([scale, scale], -1)
    cs, sn = np.cos(freqs), np.sin(freqs)
    return (np.float32(cs * scale), np.float32(sn * scale),
            np.float32(cs / scale), np.float32(sn / scale))


def _rope_tables_l0attn():
    n, HD = 2048, 64
    inv_freq = 1.0 / 10000.0 ** (np.arange(0, HD, 2, dtype=np.float64) / HD)
    t = np.arange(n, dtype=np.float64)
    freqs = t[:, None] * inv_freq[None, :]
    return np.float32(np.cos(freqs)), np.float32(np.sin(freqs))


def _masks(strict):
    i = np.arange(128)[:, None]
    j = np.arange(384)[None, :]
    lo = (i + 1) if strict else i
    mC = np.where((j >= lo) & (j <= i + 256), 0.0, -1e30).astype(np.float32)
    j2 = np.arange(256)[None, :]
    lo2 = (i - 127) if strict else (i - 128)
    mB = np.where((j2 >= lo2) & (j2 <= i + 128), 0.0, -1e30).astype(np.float32)
    j1 = np.arange(128)[None, :]
    mA = np.where(j1 <= i, 0.0, -1e30).astype(np.float32)
    return mC, mB, mA


def _tile_rearr(ap):
    return ap.rearrange("(m p) f -> p m f", p=128)


def build_PL():
    p = P("PL")
    x = p.dram_in("x", (T, D))
    z = p.dram_out("z", (T, D))
    xs = p.sb((128, D))
    out = p.sb((128, D))
    scr = p.sb((128, D))
    stat = p.sb((128, 8))
    for m in range(MT):
        r = slice(m * 128, (m + 1) * 128)
        p.dma(xs, x[r, :])
        emit_ln(p, xs, out, scr, stat, D)
        p.dma(z[r, :], out)
    return p.build()


def build_PX():
    p = P("PX")
    x = p.dram_in("x", (T, D))
    z = p.dram_in("z", (T, D))
    xo = p.dram_out("xo", (T, D))
    xs = p.sb((128, D))
    zs = p.sb((128, D))
    for m in range(MT):
        r = slice(m * 128, (m + 1) * 128)
        p.dma(xs, x[r, :])
        p.dma(zs, z[r, :])
        p.stt(xs, zs, 0.5, xs, ALU.mult, ALU.add)
        p.dma(xo[r, :], xs)
    return p.build()


def _emit_shift_load(p, dst, z, m):
    r = slice(m * 128, (m + 1) * 128)
    p.dma(dst[:, 0:512], z[r, 0:512])
    if m == 0:
        p.memset(dst[0:1, 512:1024], 0.0)
        p.dma(dst[1:128, 512:1024], z[0:127, 512:1024])
    else:
        p.dma(dst[:, 512:1024], z[m * 128 - 1:m * 128 + 127, 512:1024])


def _emit_castT(p, src_f32, bf_scr, ident, psT, dst_tiles, dst_col, nk, cast=True):
    if cast:
        p.vcopy(bf_scr, src_f32)
    else:
        bf_scr = src_f32
    for k in range(nk):
        p.trans(psT, bf_scr[:, k * 128:(k + 1) * 128], ident)
        p.vcopy(dst_tiles[k][:, dst_col:dst_col + 128], psT)


def _emit_load_w_bf16(p, wdram, wstage, tiles, K, N):
    nk = (K + 127) // 128
    for k in range(nk):
        rows = min(128, K - k * 128)
        p.dma(wstage[0:rows, 0:N], wdram[k * 128:k * 128 + rows, 0:N])
        p.vcopy(tiles[k][0:rows, 0:N], wstage[0:rows, 0:N])


def build_PA():
    p = P("PA")
    z = p.dram_in("z", (T, D))
    Wqkv = p.dram_in("Wqkv", (D, 3072))
    Wo = p.dram_in("Wo", (D, D))
    qs_in = p.dram_in("qs", (64,))
    ks_in = p.dram_in("ks", (64,))
    csq_in = p.dram_in("csq", (T, 64))
    ssq_in = p.dram_in("ssq", (T, 64))
    csk_in = p.dram_in("csk", (T, 64))
    ssk_in = p.dram_in("ssk", (T, 64))
    mC_in = p.dram_in("mC", (128, 384))
    mB_in = p.dram_in("mB", (128, 256))
    mA_in = p.dram_in("mA", (128, 128))
    ident_in = p.dram_in("ident", (128, 128), BF16)
    zo = p.dram_out("zo", (T, D))

    ident = p.sb((128, 128), BF16)
    p.dma(ident, ident_in)
    arena = p.sb((128, 24576), BF16)
    Wqkvb = [arena[:, k * 3072:(k + 1) * 3072] for k in range(8)]
    attn_oT = [arena[:, i * 2048:(i + 1) * 2048] for i in range(8)]
    Wob = [arena[:, 16384 + k * 1024:16384 + (k + 1) * 1024] for k in range(8)]
    qT = [p.sb((128, T), BF16) for _ in range(8)]
    kT = [p.sb((128, T), BF16) for _ in range(8)]
    vb = [p.sb((128, D), BF16) for _ in range(MT)]
    tabs = {}
    for nm, ap_in in (("csq", csq_in), ("ssq", ssq_in), ("csk", csk_in), ("ssk", ssk_in)):
        t_ = p.sb((128, 1024))
        p.dma(t_.rearrange("p (m f) -> p m f", f=64), _tile_rearr(ap_in))
        tabs[nm] = t_
    qsb = p.sb((128, 64))
    ksb = p.sb((128, 64))
    p.dma(qsb, qs_in[None, :].broadcast_to([128, 64]))
    p.dma(ksb, ks_in[None, :].broadcast_to([128, 64]))
    masks = p.sb((128, 768))
    p.dma(masks[:, 0:384], mC_in)
    p.dma(masks[:, 384:640], mB_in)
    p.dma(masks[:, 640:768], mA_in)

    qkv = p.sb((128, 3072))
    wstage = qkv
    _emit_load_w_bf16(p, Wqkv, wstage, Wqkvb, D, 3072)

    zs = p.sb((128, D))
    zbf = p.sb((128, D), BF16)
    scr = p.sb((128, D))
    tmp = p.sb((128, D))
    t2 = p.sb((128, 512))
    hs = p.sb((128, 64))
    psT = p.ps((128, 128), BF16)
    psA = p.ps((128, 512))
    psS = p.ps((128, 384))
    psAV = p.ps((64, 128))
    zT_m = [p.sb((128, 128), BF16) for _ in range(8)]

    def l2norm_scale(qv, sc_b):
        v3 = qv.rearrange("p (h f) -> p h f", f=64)
        p.act(scr, qv, AF.Square)
        p.red(hs[:, 0:16], scr.rearrange("p (h f) -> p h f", f=64), ALU.add)
        p.act(hs[:, 16:32], hs[:, 0:16], AF.Sqrt)
        p.op("vector", lambda e: e.tensor_scalar_max(hs[:, 16:32], hs[:, 16:32], 1e-12))
        p.recip(hs[:, 32:48], hs[:, 16:32])
        p.tt(v3, v3, hs[:, 32:48].rearrange("p h -> p h 1" if False else "p (h o) -> p h o", o=1).broadcast_to([128, 16, 64]), ALU.mult)
        p.tt(v3, v3, sc_b[:, None, :].broadcast_to([128, 16, 64]), ALU.mult)

    def rope(qv, cs_m, sn_m):
        v3 = qv.rearrange("p (h f) -> p h f", f=64)
        t3 = tmp.rearrange("p (h f) -> p h f", f=64)
        cb = cs_m[:, None, :].broadcast_to([128, 16, 64])
        p.tt(t3, v3, cb, ALU.mult)
        A = (slice(None), slice(None), slice(0, 32))
        B = (slice(None), slice(None), slice(32, 64))
        t2v = t2.rearrange("p (h f) -> p h f", f=32)
        snA = sn_m[:, None, 0:32].broadcast_to([128, 16, 32])
        snB = sn_m[:, None, 32:64].broadcast_to([128, 16, 32])
        p.tt(t2v, v3[B], snA, ALU.mult)
        p.tt(t3[A], t3[A], t2v, ALU.subtract)
        p.tt(t2v, v3[A], snB, ALU.mult)
        p.tt(t3[B], t3[B], t2v, ALU.add)

    for m in range(MT):
        _emit_shift_load(p, zs, z, m)
        _emit_castT(p, zs, zbf, ident, psT, zT_m, 0, 8)
        for n in range(6):
            for k in range(8):
                p.mm(psA, zT_m[k], Wqkvb[k][:, n * 512:(n + 1) * 512],
                     start=(k == 0), stop=(k == 7))
            p.vcopy(qkv[:, n * 512:(n + 1) * 512], psA)
        for (off, scb, csn, ssn, dstT) in ((0, qsb, "csq", "ssq", qT),
                                           (1024, ksb, "csk", "ssk", kT)):
            qv = qkv[:, off:off + 1024]
            l2norm_scale(qv, scb)
            rope(qv, tabs[csn][:, m * 64:(m + 1) * 64], tabs[ssn][:, m * 64:(m + 1) * 64])
            _emit_castT(p, tmp, zbf, ident, psT, dstT, m * 128, 8)
        p.vcopy(vb[m], qkv[:, 2048:3072])

    e32 = p.sb((128, 384))
    ebf = p.sb((128, 384), BF16)
    eT = [p.sb((128, 128), BF16) for _ in range(3)]
    sstat = p.sb((128, 8))
    for m in range(MT):
        lo = max(0, m * 128 - 256)
        width = m * 128 + 128 - lo
        nj = width // 128
        mask = {384: masks[:, 0:384], 256: masks[:, 384:640],
                128: masks[:, 640:768]}[width]
        for h in range(16):
            kt = h // 2
            r0 = (h % 2) * 64
            p.mm(psS[:, 0:width], qT[kt][r0:r0 + 64, m * 128:(m + 1) * 128],
                 kT[kt][r0:r0 + 64, lo:lo + width], start=True, stop=True)
            p.stt(e32[:, 0:width], psS[:, 0:width], 8.0, mask, ALU.mult, ALU.add)
            p.red(sstat[:, 0:1], e32[:, 0:width], ALU.max, negate=True)
            p.act(e32[:, 0:width], e32[:, 0:width], AF.Exp, bias=sstat[:, 0:1],
                  accum_out=sstat[:, 1:2])
            p.recip(sstat[:, 2:3], sstat[:, 1:2])
            p.ts(ebf[:, 0:width], e32[:, 0:width], sstat[:, 2:3], ALU.mult)
            for j in range(nj):
                p.trans(psT, ebf[:, j * 128:(j + 1) * 128], ident)
                p.vcopy(eT[j], psT)
            for j in range(nj):
                jt = lo // 128 + j
                p.mm(psAV, vb[jt][:, h * 64:h * 64 + 64], eT[j],
                     start=(j == 0), stop=(j == nj - 1))
            p.vcopy(attn_oT[h // 2][r0:r0 + 64, m * 128:(m + 1) * 128], psAV)

    _emit_load_w_bf16(p, Wo, wstage, Wob, D, D)
    out = tmp
    for m in range(MT):
        r = slice(m * 128, (m + 1) * 128)
        for n in range(2):
            for k in range(8):
                p.mm(psA, attn_oT[k][:, r], Wob[k][:, n * 512:(n + 1) * 512],
                     start=(k == 0), stop=(k == 7))
            p.vcopy(out[:, n * 512:(n + 1) * 512], psA)
        p.dma(zs, z[r, :])
        p.tt(out, out, zs, ALU.add)
        p.dma(zo[r, :], out)
    return p.build()


def build_PF():
    DI = 2730
    p = P("PF")
    z = p.dram_in("z", (T, D))
    W1 = p.dram_in("W1", (D, 2 * DI))
    b1 = p.dram_in("b1", (2 * DI,))
    W2 = p.dram_in("W2", (DI, D))
    b2 = p.dram_in("b2", (D,))
    g_in = p.dram_in("g", (D,))
    bl_in = p.dram_in("bl", (D,))
    ident_in = p.dram_in("ident", (128, 128), BF16)
    zo = p.dram_out("zo", (T, D))

    ident = p.sb((128, 128), BF16)
    p.dma(ident, ident_in)
    W1b = [p.sb((128, 2 * DI), BF16) for _ in range(8)]
    kt2 = [(k * 128, min(128, DI - k * 128)) for k in range((DI + 127) // 128)]
    W2b = [p.sb((128, D), BF16) for _ in range(len(kt2))]
    b1T = p.sb((128, 2 * DI))
    gT = p.sb((128, D))
    blT = p.sb((128, D))
    b2T = p.sb((128, D))
    p.dma(b1T, b1[None, :].broadcast_to([128, 2 * DI]))
    p.dma(gT, g_in[None, :].broadcast_to([128, D]))
    p.dma(blT, bl_in[None, :].broadcast_to([128, D]))
    p.dma(b2T, b2[None, :].broadcast_to([128, D]))

    wstage = p.sb((128, 2 * DI))
    _emit_load_w_bf16(p, W1, wstage, W1b, D, 2 * DI)
    for k, (k0, rows) in enumerate(kt2):
        p.dma(wstage[0:rows, 0:D], W2[k0:k0 + rows, :])
        p.vcopy(W2b[k][0:rows, :], wstage[0:rows, 0:D])

    zs = wstage[:, 0:1024]
    xn = wstage[:, 1024:2048]
    scr = wstage[:, 2048:3072]
    out = wstage[:, 3072:4096]
    zres = wstage[:, 4096:5120]
    stat = p.sb((128, 8))
    xnb = p.sb((128, D), BF16)
    xT_m = [p.sb((128, 128), BF16) for _ in range(8)]
    aa = p.sb((128, 512))
    gg = p.sb((128, 512))
    prodf = p.sb((128, 512))
    prodb = p.sb((128, 512), BF16)
    prodT = [p.sb((128, 128), BF16) for _ in range(len(kt2))]
    psA = p.ps((128, 512))
    psG = p.ps((128, 512))
    psT = p.ps((128, 128), BF16)

    chunks = [(c * 512, min(512, DI - c * 512)) for c in range((DI + 511) // 512)]
    for m in range(MT):
        r = slice(m * 128, (m + 1) * 128)
        _emit_shift_load(p, zs, z, m)
        emit_ln(p, zs, xn, scr, stat, D)
        p.tt(xn, xn, gT, ALU.mult)
        p.tt(xn, xn, blT, ALU.add)
        _emit_castT(p, xn, xnb, ident, psT, xT_m, 0, 8)
        for (c0, Na) in chunks:
            for k in range(8):
                p.mm(psA[:, 0:Na], xT_m[k], W1b[k][:, c0:c0 + Na],
                     start=(k == 0), stop=(k == 7))
            for k in range(8):
                p.mm(psG[:, 0:Na], xT_m[k], W1b[k][:, DI + c0:DI + c0 + Na],
                     start=(k == 0), stop=(k == 7))
            p.tt(aa[:, 0:Na], psA[:, 0:Na], b1T[:, c0:c0 + Na], ALU.add)
            p.tt(gg[:, 0:Na], psG[:, 0:Na], b1T[:, DI + c0:DI + c0 + Na], ALU.add)
            p.act(gg[:, 0:Na], gg[:, 0:Na], AF.Gelu)
            p.tt(prodf[:, 0:Na], aa[:, 0:Na], gg[:, 0:Na], ALU.mult)
            p.vcopy(prodb[:, 0:Na], prodf[:, 0:Na])
            j0 = 0
            while j0 < Na:
                bw = min(128, Na - j0)
                kidx = (c0 + j0) // 128
                p.trans(psT[0:bw, :], prodb[:, j0:j0 + bw], ident)
                p.vcopy(prodT[kidx][0:bw, :], psT[0:bw, :])
                j0 += bw
        for n in range(2):
            for ki, (k0, rows) in enumerate(kt2):
                p.mm(psA, prodT[ki][0:rows, :], W2b[ki][0:rows, n * 512:(n + 1) * 512],
                     start=(ki == 0), stop=(ki == len(kt2) - 1))
            p.tt(out[:, n * 512:(n + 1) * 512], psA, b2T[:, n * 512:(n + 1) * 512], ALU.add)
        p.dma(zres, z[r, :])
        p.tt(out, out, zres, ALU.add)
        p.dma(zo[r, :], out)
    return p.build()


def build_PA0():
    p = P("PA0")
    x = p.dram_in("x", (T, D))
    Wqkv = p.dram_in("Wqkv", (D, 1536))
    Wout = p.dram_in("Wout", (512, D))
    cos_in = p.dram_in("cos", (T, 32))
    sin_in = p.dram_in("sin", (T, 32))
    mC_in = p.dram_in("mC", (128, 384))
    mB_in = p.dram_in("mB", (128, 256))
    mA_in = p.dram_in("mA", (128, 128))
    ident_in = p.dram_in("ident", (128, 128), BF16)
    xo = p.dram_out("xo", (T, D))

    ident = p.sb((128, 128), BF16)
    p.dma(ident, ident_in)
    Wqkvb = [p.sb((128, 1536), BF16) for _ in range(8)]
    Woutb = [p.sb((128, D), BF16) for _ in range(4)]
    qT = [p.sb((128, T), BF16) for _ in range(4)]
    kT = [p.sb((128, T), BF16) for _ in range(4)]
    vb = [p.sb((128, 512), BF16) for _ in range(MT)]
    attn_oT = [p.sb((128, T), BF16) for _ in range(4)]
    cosb = p.sb((128, 512))
    sinb = p.sb((128, 512))
    p.dma(cosb.rearrange("p (m f) -> p m f", f=32), _tile_rearr(cos_in))
    p.dma(sinb.rearrange("p (m f) -> p m f", f=32), _tile_rearr(sin_in))
    masks = p.sb((128, 768))
    p.dma(masks[:, 0:384], mC_in)
    p.dma(masks[:, 384:640], mB_in)
    p.dma(masks[:, 640:768], mA_in)

    qkv = p.sb((128, 1536))
    wstage = p.sb((128, 1536))
    _emit_load_w_bf16(p, Wqkv, wstage, Wqkvb, D, 1536)
    _emit_load_w_bf16(p, Wout, wstage, Woutb, 512, D)

    xs = p.sb((128, D))
    xn = p.sb((128, D))
    scr = p.sb((128, D))
    tmp = p.sb((128, 512))
    t2 = p.sb((128, 256))
    stat = p.sb((128, 8))
    xnb = p.sb((128, D), BF16)
    xT_m = [p.sb((128, 128), BF16) for _ in range(8)]
    psT = p.ps((128, 128), BF16)
    psA = p.ps((128, 512))
    psS = p.ps((128, 384))
    psAV = p.ps((64, 128))

    def rope0(qv, cs_m, sn_m):
        v4 = qv.rearrange("p (h t f) -> p h t f", t=2, f=32)
        t4 = tmp.rearrange("p (h t f) -> p h t f", t=2, f=32)
        cb4 = cs_m[:, None, None, :].broadcast_to([128, 8, 2, 32])
        p.tt(t4, v4, cb4, ALU.mult)
        v3 = qv.rearrange("p (h f) -> p h f", f=64)
        t3 = tmp.rearrange("p (h f) -> p h f", f=64)
        t2v = t2.rearrange("p (h f) -> p h f", f=32)
        sb3 = sn_m[:, None, :].broadcast_to([128, 8, 32])
        A = (slice(None), slice(None), slice(0, 32))
        B = (slice(None), slice(None), slice(32, 64))
        p.tt(t2v, v3[B], sb3, ALU.mult)
        p.tt(t3[A], t3[A], t2v, ALU.subtract)
        p.tt(t2v, v3[A], sb3, ALU.mult)
        p.tt(t3[B], t3[B], t2v, ALU.add)

    for m in range(MT):
        r = slice(m * 128, (m + 1) * 128)
        p.dma(xs, x[r, :])
        emit_ln(p, xs, xn, scr, stat, D)
        _emit_castT(p, xn, xnb, ident, psT, xT_m, 0, 8)
        for n in range(3):
            for k in range(8):
                p.mm(psA, xT_m[k], Wqkvb[k][:, n * 512:(n + 1) * 512],
                     start=(k == 0), stop=(k == 7))
            p.vcopy(qkv[:, n * 512:(n + 1) * 512], psA)
        cs_m = cosb[:, m * 32:(m + 1) * 32]
        sn_m = sinb[:, m * 32:(m + 1) * 32]
        for (off, dstT) in ((0, qT), (512, kT)):
            rope0(qkv[:, off:off + 512], cs_m, sn_m)
            _emit_castT(p, tmp, xnb[:, 0:512], ident, psT, dstT, m * 128, 4)
        p.vcopy(vb[m], qkv[:, 1024:1536])

    e32 = p.sb((128, 384))
    ebf = p.sb((128, 384), BF16)
    eT = [p.sb((128, 128), BF16) for _ in range(3)]
    sstat = p.sb((128, 8))
    for m in range(MT):
        lo = max(0, m * 128 - 256)
        width = m * 128 + 128 - lo
        nj = width // 128
        mask = {384: masks[:, 0:384], 256: masks[:, 384:640],
                128: masks[:, 640:768]}[width]
        for h in range(8):
            kt = h // 2
            r0 = (h % 2) * 64
            p.mm(psS[:, 0:width], qT[kt][r0:r0 + 64, m * 128:(m + 1) * 128],
                 kT[kt][r0:r0 + 64, lo:lo + width], start=True, stop=True)
            p.stt(e32[:, 0:width], psS[:, 0:width], 0.125, mask, ALU.mult, ALU.add)
            p.red(sstat[:, 0:1], e32[:, 0:width], ALU.max, negate=True)
            p.act(e32[:, 0:width], e32[:, 0:width], AF.Exp, bias=sstat[:, 0:1],
                  accum_out=sstat[:, 1:2])
            p.recip(sstat[:, 2:3], sstat[:, 1:2])
            p.ts(ebf[:, 0:width], e32[:, 0:width], sstat[:, 2:3], ALU.mult)
            for j in range(nj):
                p.trans(psT, ebf[:, j * 128:(j + 1) * 128], ident)
                p.vcopy(eT[j], psT)
            for j in range(nj):
                jt = lo // 128 + j
                p.mm(psAV, vb[jt][:, h * 64:h * 64 + 64], eT[j],
                     start=(j == 0), stop=(j == nj - 1))
            p.vcopy(attn_oT[h // 2][r0:r0 + 64, m * 128:(m + 1) * 128], psAV)

    for m in range(MT):
        r = slice(m * 128, (m + 1) * 128)
        for n in range(2):
            for k in range(4):
                p.mm(psA, attn_oT[k][:, r], Woutb[k][:, n * 512:(n + 1) * 512],
                     start=(k == 0), stop=(k == 3))
            p.vcopy(xn[:, n * 512:(n + 1) * 512], psA)
        p.dma(xs, x[r, :])
        p.tt(xn, xn, xs, ALU.add)
        p.dma(xo[r, :], xn)
    return p.build()


def build_PQ1():
    H, F = 1024, 4096
    p = P("PQ1")
    x = p.dram_in("x", (T, D))
    gamma = p.dram_in("gamma", (D,))
    beta = p.dram_in("beta", (D,))
    q1 = p.dram_in("q1", (H, F), I32)
    s1 = p.dram_in("s1", (8, F))
    bb1 = p.dram_in("bb1", (F,))
    q2 = p.dram_in("q2", (F, H), I32)
    s2 = p.dram_in("s2", (32, H))
    bb2 = p.dram_in("bb2", (H,))
    ident_in = p.dram_in("ident", (128, 128), BF16)
    xo = p.dram_out("xo", (T, D))

    ident = p.sb((128, 128), BF16)
    p.dma(ident, ident_in)
    W1b = [p.sb((128, F), BF16) for _ in range(8)]
    W2b = [p.sb((128, H), BF16) for _ in range(32)]
    gT = p.sb((128, D))
    bT = p.sb((128, D))
    b1T = p.sb((128, F))
    b2T = p.sb((128, H))
    p.dma(gT, gamma[None, :].broadcast_to([128, D]))
    p.dma(bT, beta[None, :].broadcast_to([128, D]))
    p.dma(b1T, bb1[None, :].broadcast_to([128, F]))
    p.dma(b2T, bb2[None, :].broadcast_to([128, H]))

    qstage = p.sb((128, 512), I32)
    sstage = p.sb((128, 512))
    fstage = p.sb((128, F))
    for (qd, sd, wt, nk, NN) in ((q1, s1, W1b, 8, F), (q2, s2, W2b, 32, H)):
        for k in range(nk):
            for c in range(NN // 512):
                cs_ = slice(c * 512, (c + 1) * 512)
                p.dma(qstage, qd[k * 128:(k + 1) * 128, cs_])
                p.dma(sstage, sd[k:k + 1, cs_].broadcast_to([128, 512]))
                p.vcopy(fstage[:, 0:512], qstage)
                p.ts(fstage[:, 0:512], fstage[:, 0:512], -128.0, ALU.add)
                p.tt(wt[k][:, cs_], fstage[:, 0:512], sstage, ALU.mult)

    xs = p.sb((128, D))
    xn = p.sb((128, D))
    lnr = p.sb((128, D))
    scr = p.sb((128, D))
    stat = p.sb((128, 8))
    xnb = p.sb((128, D), BF16)
    xT_m = [p.sb((128, 128), BF16) for _ in range(8)]
    hg = fstage
    hgb = p.sb((128, 512), BF16)
    hT_m = [p.sb((128, 128), BF16) for _ in range(32)]
    psA = p.ps((128, 512))
    psT = p.ps((128, 128), BF16)

    for m in range(MT):
        r = slice(m * 128, (m + 1) * 128)
        p.dma(xs, x[r, :])
        emit_ln(p, xs, lnr, scr, stat, D)
        p.tt(xn, lnr, gT, ALU.mult)
        p.tt(xn, xn, bT, ALU.add)
        _emit_castT(p, xn, xnb, ident, psT, xT_m, 0, 8)
        for n in range(8):
            c = slice(n * 512, (n + 1) * 512)
            for k in range(8):
                p.mm(psA, xT_m[k], W1b[k][:, c], start=(k == 0), stop=(k == 7))
            p.tt(hg[:, c], psA, b1T[:, c], ALU.add)
        p.act(hg, hg, AF.Silu)
        for c8 in range(8):
            p.vcopy(hgb, hg[:, c8 * 512:(c8 + 1) * 512])
            for j in range(4):
                p.trans(psT, hgb[:, j * 128:(j + 1) * 128], ident)
                p.vcopy(hT_m[c8 * 4 + j], psT)
        for n in range(2):
            c = slice(n * 512, (n + 1) * 512)
            for k in range(32):
                p.mm(psA, hT_m[k], W2b[k][:, c], start=(k == 0), stop=(k == 31))
            p.tt(scr[:, c], psA, b2T[:, c], ALU.add)
        p.stt(scr, scr, 0.25, xs, ALU.mult, ALU.add)
        p.stt(xn, lnr, 0.5, scr, ALU.mult, ALU.add)
        p.dma(xo[r, :], xn)
    return p.build()


def build_PS1():
    N = 1024
    p = P("PS1")
    x = p.dram_in("x", (T, D))
    rmsw = p.dram_in("rmsw", (D,))
    wA = p.dram_in("wA", (D, N))
    bA = p.dram_in("bA", (N,))
    wx = p.dram_in("wx", (D, N))
    wsg = p.dram_in("wsg", (D, N))
    wout = p.dram_in("wout", (D, N))
    wo = p.dram_in("wo", (N, D))
    ident_in = p.dram_in("ident", (128, 128), BF16)
    xo = p.dram_out("xo", (T, D))

    ident = p.sb((128, 128), BF16)
    p.dma(ident, ident_in)
    wAb = [p.sb((128, N), BF16) for _ in range(8)]
    wxb = [p.sb((128, N), BF16) for _ in range(8)]
    wsgb = [p.sb((128, N), BF16) for _ in range(8)]
    woutb = [p.sb((128, N), BF16) for _ in range(8)]
    wob = [p.sb((128, D), BF16) for _ in range(8)]
    wstage = p.sb((128, N))
    for wd, wt in ((wA, wAb), (wx, wxb), (wsg, wsgb), (wout, woutb), (wo, wob)):
        _emit_load_w_bf16(p, wd, wstage, wt, D, N)
    rmsT = p.sb((128, D))
    p.dma(rmsT, rmsw[None, :].broadcast_to([128, D]))
    bA_c = p.sb((128, 8))
    for c in range(8):
        p.dma(bA_c[:, c:c + 1],
              bA[c * 128:(c + 1) * 128].rearrange("(p o) -> p o", o=1))

    xs = p.sb((128, D))
    xn = p.sb((128, D))
    scr = p.sb((128, D))
    sq = p.sb((128, D))
    stat = p.sb((128, 8))
    xnb = p.sb((128, D), BF16)
    xnT = [p.sb((128, T), BF16) for _ in range(8)]
    psA = p.ps((128, 512))
    psT = p.ps((128, 128), BF16)

    for m in range(MT):
        r = slice(m * 128, (m + 1) * 128)
        p.dma(xs, x[r, :])
        emit_ln(p, xs, xn, scr, stat, D)
        emit_rms(p, xn, scr, sq, stat, D, 1e-5)
        p.tt(scr, scr, rmsT, ALU.mult)
        _emit_castT(p, scr, xnb, ident, psT, xnT, m * 128, 8)

    A_c = p.sb((128, T))
    b_c = p.sb((128, T))
    g_c = p.sb((128, T))
    h_c = p.sb((128, T))
    yT = [p.sb((128, T), BF16) for _ in range(8)]
    for c in range(8):
        cc = slice(c * 128, (c + 1) * 128)
        for nch in range(4):
            ns = slice(nch * 512, (nch + 1) * 512)
            for k in range(8):
                p.mm(psA, wAb[k][:, cc], xnT[k][:, ns], start=(k == 0), stop=(k == 7))
            p.act(A_c[:, ns], psA, AF.Sigmoid, bias=bA_c[:, c:c + 1])
            for k in range(8):
                p.mm(psA, wxb[k][:, cc], xnT[k][:, ns], start=(k == 0), stop=(k == 7))
            p.vcopy(b_c[:, ns], psA)
            for k in range(8):
                p.mm(psA, woutb[k][:, cc], xnT[k][:, ns], start=(k == 0), stop=(k == 7))
            p.act(h_c[:, ns], psA, AF.Sigmoid)
            for k in range(8):
                p.mm(psA, wsgb[k][:, cc], xnT[k][:, ns], start=(k == 0), stop=(k == 7))
            p.act(g_c[:, ns], psA, AF.Sigmoid)
        p.tt(b_c, b_c, h_c, ALU.mult)
        for s in range(4):
            ss = slice(s * 512, (s + 1) * 512)
            init = 0.0 if s == 0 else h_c[:, s * 512 - 1:s * 512]
            p.scan(h_c[:, ss], A_c[:, ss], b_c[:, ss], init)
        p.tt(h_c, g_c, h_c, ALU.mult)
        p.vcopy(yT[c], h_c)

    for m in range(MT):
        r = slice(m * 128, (m + 1) * 128)
        p.dma(xs, x[r, :])
        emit_ln(p, xs, xn, scr, stat, D)
        p.tt(xn, xn, xs, ALU.add)
        for n in range(2):
            c = slice(n * 512, (n + 1) * 512)
            for k in range(8):
                p.mm(psA, yT[k][:, r], wob[k][:, c], start=(k == 0), stop=(k == 7))
            p.tt(xn[:, c], xn[:, c], psA, ALU.add)
        p.dma(xo[r, :], xn)
    return p.build()


def build_PM2a():
    p = P("PM2a")
    x = p.dram_in("x", (T, D))
    rw = p.dram_in("rw", (8, D))
    eb = p.dram_in("eb", (8,))
    sw1 = p.dram_in("sw1", (D, 2048))
    sw3 = p.dram_in("sw3", (D, 2048))
    sw2 = p.dram_in("sw2", (2048, D))
    ident_in = p.dram_in("ident", (128, 128), BF16)
    acc = p.dram_out("acc", (T, D))
    xn_o = p.dram_out("xn", (T, D))
    cw_o = p.dram_out("cw", (T, 8))

    ident = p.sb((128, 128), BF16)
    p.dma(ident, ident_in)
    sw1b = [p.sb((128, 2048), BF16) for _ in range(8)]
    sw3b = [p.sb((128, 2048), BF16) for _ in range(8)]
    sw2b = [p.sb((128, D), BF16) for _ in range(16)]
    wstage = p.sb((128, 2048))
    _emit_load_w_bf16(p, sw1, wstage, sw1b, D, 2048)
    _emit_load_w_bf16(p, sw3, wstage, sw3b, D, 2048)
    _emit_load_w_bf16(p, sw2, wstage, sw2b, 2048, D)
    rw8 = p.sb((8, D))
    rw8b = p.sb((8, D), BF16)
    p.dma(rw8, rw)
    p.vcopy(rw8b, rw8)
    psT8 = p.ps((128, 8), BF16)
    rwT = [p.sb((128, 8), BF16) for _ in range(8)]
    for k in range(8):
        p.trans(psT8, rw8b[:, k * 128:(k + 1) * 128], ident[0:8, 0:8])
        p.vcopy(rwT[k], psT8)
    ebT = p.sb((128, 8))
    p.dma(ebT, eb[None, :].broadcast_to([128, 8]))

    xs = p.sb((128, D))
    xn = p.sb((128, D))
    scr = p.sb((128, D))
    stat = p.sb((128, 8))
    xnb = p.sb((128, D), BF16)
    xT_m = [p.sb((128, 128), BF16) for _ in range(8)]
    sg = wstage
    su = p.sb((128, 2048))
    actb = p.sb((128, 2048), BF16)
    actT = [p.sb((128, 128), BF16) for _ in range(16)]
    lg = p.sb((128, 8))
    tt8 = p.sb((128, 48))
    psA = p.ps((128, 512))
    psT = p.ps((128, 128), BF16)
    ps8 = p.ps((128, 8))

    for m in range(MT):
        r = slice(m * 128, (m + 1) * 128)
        p.dma(xs, x[r, :])
        emit_ln(p, xs, xn, scr, stat, D)
        p.dma(xn_o[r, :], xn)
        _emit_castT(p, xn, xnb, ident, psT, xT_m, 0, 8)
        for k in range(8):
            p.mm(ps8, xT_m[k], rwT[k], start=(k == 0), stop=(k == 7))
        p.tt(lg, ps8, ebT, ALU.add)
        m1, m2 = stat[:, 4:5], stat[:, 5:6]
        mask1, mask2, msk = tt8[:, 0:8], tt8[:, 8:16], tt8[:, 16:24]
        dd, ee, p1a, p2a = stat[:, 6:7], tt8[:, 24:25], tt8[:, 25:26], tt8[:, 26:27]
        p.red(m1, lg, ALU.max)
        p.ts(mask1, lg, m1, ALU.is_ge)
        p.stt(msk, mask1, -1e30, lg, ALU.mult, ALU.add)
        p.red(m2, msk, ALU.max)
        p.ts(mask2, msk, m2, ALU.is_ge)
        p.tt(dd, m2, m1, ALU.subtract)
        p.act(ee, dd, AF.Exp)
        p.ts(ee, ee, 1.0, ALU.add)
        p.recip(p1a, ee)
        p.ts2(p2a, p1a, -1.0, ALU.mult, 1.0, ALU.add)
        p.ts(tt8[:, 32:40], mask1, p1a, ALU.mult)
        p.stt(lg, mask2, p2a, tt8[:, 32:40], ALU.mult, ALU.add)
        p.dma(cw_o[r, :], lg)
        for n in range(4):
            ns = slice(n * 512, (n + 1) * 512)
            for k in range(8):
                p.mm(psA, xT_m[k], sw1b[k][:, ns], start=(k == 0), stop=(k == 7))
            p.act(sg[:, ns], psA, AF.Silu)
            for k in range(8):
                p.mm(psA, xT_m[k], sw3b[k][:, ns], start=(k == 0), stop=(k == 7))
            p.vcopy(su[:, ns], psA)
        p.tt(su, sg, su, ALU.mult)
        p.vcopy(actb, su)
        for j in range(16):
            p.trans(psT, actb[:, j * 128:(j + 1) * 128], ident)
            p.vcopy(actT[j], psT)
        for n in range(2):
            ns = slice(n * 512, (n + 1) * 512)
            for k in range(16):
                p.mm(psA, actT[k], sw2b[k][:, ns], start=(k == 0), stop=(k == 15))
            p.stt(scr[:, ns], psA, 0.5, xs[:, ns], ALU.mult, ALU.add)
        p.dma(acc[r, :], scr)
    return p.build()


def build_PM2b():
    p = P("PM2b")
    xn_i = p.dram_in("xn", (T, D))
    cw1 = p.dram_in("cw1", (T, 1))
    acc_i = p.dram_in("acc_in", (T, D))
    w1p = p.dram_in("w1p", (D, 2048))
    w3p = p.dram_in("w3p", (D, 2048))
    w2p = p.dram_in("w2p", (2048, D))
    ident_in = p.dram_in("ident", (128, 128), BF16)
    acc = p.dram_out("acc", (T, D))

    ident = p.sb((128, 128), BF16)
    p.dma(ident, ident_in)
    w1b = [p.sb((128, 2048), BF16) for _ in range(8)]
    w3b = [p.sb((128, 2048), BF16) for _ in range(8)]
    w2b = [p.sb((128, D), BF16) for _ in range(16)]
    wstage = p.sb((128, 2048))
    _emit_load_w_bf16(p, w1p, wstage, w1b, D, 2048)
    _emit_load_w_bf16(p, w3p, wstage, w3b, D, 2048)
    _emit_load_w_bf16(p, w2p, wstage, w2b, 2048, D)

    xn = p.sb((128, D))
    am = p.sb((128, D))
    cwm = p.sb((128, 1))
    xnb = p.sb((128, D), BF16)
    xT_m = [p.sb((128, 128), BF16) for _ in range(8)]
    sg = wstage
    su = p.sb((128, 2048))
    actb = p.sb((128, 2048), BF16)
    actT = [p.sb((128, 128), BF16) for _ in range(16)]
    psA = p.ps((128, 512))
    psT = p.ps((128, 128), BF16)

    for m in range(MT):
        r = slice(m * 128, (m + 1) * 128)
        p.dma(xn, xn_i[r, :])
        p.dma(cwm, cw1[r, :])
        p.dma(am, acc_i[r, :])
        _emit_castT(p, xn, xnb, ident, psT, xT_m, 0, 8)
        for n in range(4):
            ns = slice(n * 512, (n + 1) * 512)
            for k in range(8):
                p.mm(psA, xT_m[k], w1b[k][:, ns], start=(k == 0), stop=(k == 7))
            p.act(sg[:, ns], psA, AF.Silu)
            for k in range(8):
                p.mm(psA, xT_m[k], w3b[k][:, ns], start=(k == 0), stop=(k == 7))
            p.vcopy(su[:, ns], psA)
        p.tt(su, sg, su, ALU.mult)
        p.ts(su, su, cwm[:, 0:1], ALU.mult)
        p.vcopy(actb, su)
        for j in range(16):
            p.trans(psT, actb[:, j * 128:(j + 1) * 128], ident)
            p.vcopy(actT[j], psT)
        for n in range(2):
            ns = slice(n * 512, (n + 1) * 512)
            for k in range(16):
                p.mm(psA, actT[k], w2b[k][:, ns], start=(k == 0), stop=(k == 15))
            p.stt(am[:, ns], psA, 0.5, am[:, ns], ALU.mult, ALU.add)
        p.dma(acc[r, :], am)
    return p.build()


def build_PA2q():
    p = P("PA2q")
    x = p.dram_in("x", (T, D))
    Wqkv = p.dram_in("Wqkv", (3072, D))
    bqkv = p.dram_in("bqkv", (3072,))
    ident_in = p.dram_in("ident", (128, 128), BF16)
    qTo = p.dram_out("qT", (D, T), BF16)
    kTo = p.dram_out("kT", (D, T), BF16)
    vo = p.dram_out("v", (T, D), BF16)

    ident = p.sb((128, 128), BF16)
    p.dma(ident, ident_in)
    WqT = [p.sb((128, 3072), BF16) for _ in range(8)]
    wstage = p.sb((128, D))
    wstb = p.sb((128, D), BF16)
    for rt in range(24):
        p.dma(wstage, Wqkv[rt * 128:(rt + 1) * 128, :])
        p.vcopy(wstb, wstage)
        for k in range(8):
            p.trans(psT_ := None or psT, wstb[:, k * 128:(k + 1) * 128], ident) if False else None
    bqT = p.sb((128, 3072))
    p.dma(bqT, bqkv[None, :].broadcast_to([128, 3072]))

    xs = p.sb((128, D))
    xn = p.sb((128, D))
    scr = p.sb((128, D))
    stat = p.sb((128, 8))
    xnb = p.sb((128, D), BF16)
    xT_m = [p.sb((128, 128), BF16) for _ in range(8)]
    qkvf = p.sb((128, 3072))
    pack = p.sb((128, D))
    packb = p.sb((128, D), BF16)
    qT = [p.sb((128, T), BF16) for _ in range(8)]
    kT = [p.sb((128, T), BF16) for _ in range(8)]
    psA = p.ps((128, 512))
    psT = p.ps((128, 128), BF16)

    for rt in range(24):
        p.dma(wstage, Wqkv[rt * 128:(rt + 1) * 128, :])
        p.vcopy(wstb, wstage)
        for k in range(8):
            p.trans(psT, wstb[:, k * 128:(k + 1) * 128], ident)
            p.vcopy(WqT[k][:, rt * 128:(rt + 1) * 128], psT)

    for m in range(MT):
        r = slice(m * 128, (m + 1) * 128)
        p.dma(xs, x[r, :])
        emit_ln(p, xs, xn, scr, stat, D)
        _emit_castT(p, xn, xnb, ident, psT, xT_m, 0, 8)
        for n in range(6):
            ns = slice(n * 512, (n + 1) * 512)
            for k in range(8):
                p.mm(psA, xT_m[k], WqT[k][:, ns], start=(k == 0), stop=(k == 7))
            p.tt(qkvf[:, ns], psA, bqT[:, ns], ALU.add)
        qv = qkvf.rearrange("p (h t) -> p h t", t=192)
        for (off, dstT) in ((0, qT), (64, kT)):
            p.vcopy(pack.rearrange("p (h f) -> p h f", f=64), qv[:, :, off:off + 64])
            _emit_castT(p, pack, packb, ident, psT, dstT, m * 128, 8)
        p.vcopy(packb.rearrange("p (h f) -> p h f", f=64), qv[:, :, 128:192])
        p.dma(vo[r, :], packb)
    for k in range(8):
        p.dma(qTo[k * 128:(k + 1) * 128, :], qT[k])
        p.dma(kTo[k * 128:(k + 1) * 128, :], kT[k])
    return p.build()


def build_PA2h():
    p = P("PA2h")
    qT_i = p.dram_in("qTh", (512, T), BF16)
    kT_i = p.dram_in("kTh", (512, T), BF16)
    v_i = p.dram_in("vh", (T, 512), BF16)
    ident_in = p.dram_in("ident", (128, 128), BF16)
    oTo = p.dram_out("oTh", (512, T), BF16)

    ident = p.sb((128, 128), BF16)
    p.dma(ident, ident_in)
    qT = [p.sb((128, T), BF16) for _ in range(4)]
    kT = [p.sb((128, T), BF16) for _ in range(4)]
    vb = [p.sb((128, 512), BF16) for _ in range(MT)]
    oT = [p.sb((128, T), BF16) for _ in range(4)]
    for k in range(4):
        p.dma(qT[k], qT_i[k * 128:(k + 1) * 128, :])
        p.dma(kT[k], kT_i[k * 128:(k + 1) * 128, :])
    for m in range(MT):
        p.dma(vb[m], v_i[m * 128:(m + 1) * 128, :])

    e32 = p.sb((128, T))
    ebf = p.sb((128, T), BF16)
    eT = p.sb((128, 128), BF16)
    sstat = p.sb((128, 8))
    psS = p.ps((128, 2048))
    psT = p.ps((128, 128), BF16)
    psAV = p.ps((64, 128))

    for m in range(MT):
        r = slice(m * 128, (m + 1) * 128)
        for h in range(8):
            kt = h // 2
            r0 = (h % 2) * 64
            for n in range(4):
                ns = slice(n * 512, (n + 1) * 512)
                p.mm(psS[:, ns], qT[kt][r0:r0 + 64, r], kT[kt][r0:r0 + 64, ns],
                     start=True, stop=True)
            p.red(sstat[:, 0:1], psS, ALU.max, negate=True)
            p.ts(sstat[:, 3:4], sstat[:, 0:1], 0.125, ALU.mult)
            p.act(e32, psS, AF.Exp, bias=sstat[:, 3:4], scale=0.125,
                  accum_out=sstat[:, 1:2])
            p.recip(sstat[:, 2:3], sstat[:, 1:2])
            p.ts(ebf, e32, sstat[:, 2:3], ALU.mult)
            for j in range(MT):
                p.trans(psT, ebf[:, j * 128:(j + 1) * 128], ident)
                p.vcopy(eT, psT)
                p.mm(psAV, vb[j][:, h * 64:h * 64 + 64], eT,
                     start=(j == 0), stop=(j == MT - 1))
            p.vcopy(oT[kt][r0:r0 + 64, r], psAV)
    for k in range(4):
        p.dma(oTo[k * 128:(k + 1) * 128, :], oT[k])
    return p.build()


def build_PA2o():
    p = P("PA2o")
    x = p.dram_in("x", (T, D))
    oT0 = p.dram_in("oT0", (512, T), BF16)
    oT1 = p.dram_in("oT1", (512, T), BF16)
    Wout = p.dram_in("Wout", (D, D))
    bout = p.dram_in("bout", (D,))
    ident_in = p.dram_in("ident", (128, 128), BF16)
    xo = p.dram_out("xo", (T, D))

    ident = p.sb((128, 128), BF16)
    p.dma(ident, ident_in)
    oT = [p.sb((128, T), BF16) for _ in range(8)]
    for k in range(4):
        p.dma(oT[k], oT0[k * 128:(k + 1) * 128, :])
        p.dma(oT[4 + k], oT1[k * 128:(k + 1) * 128, :])
    WoT = [p.sb((128, D), BF16) for _ in range(8)]
    wstage = p.sb((128, D))
    wstb = p.sb((128, D), BF16)
    psT = p.ps((128, 128), BF16)
    for rt in range(8):
        p.dma(wstage, Wout[rt * 128:(rt + 1) * 128, :])
        p.vcopy(wstb, wstage)
        for k in range(8):
            p.trans(psT, wstb[:, k * 128:(k + 1) * 128], ident)
            p.vcopy(WoT[k][:, rt * 128:(rt + 1) * 128], psT)
    boT = p.sb((128, D))
    p.dma(boT, bout[None, :].broadcast_to([128, D]))
    xs = p.sb((128, D))
    out = p.sb((128, D))
    psA = p.ps((128, 512))
    for m in range(MT):
        r = slice(m * 128, (m + 1) * 128)
        p.dma(xs, x[r, :])
        for n in range(2):
            ns = slice(n * 512, (n + 1) * 512)
            for k in range(8):
                p.mm(psA, oT[k][:, r], WoT[k][:, ns], start=(k == 0), stop=(k == 7))
            p.tt(out[:, ns], psA, boT[:, ns], ALU.add)
        p.tt(out, out, xs, ALU.add)
        p.tt(out, out, boT, ALU.add) if False else None
        p.dma(xo[r, :], out)
    return p.build()


def build_PS3a1():
    DI2 = 2048
    p = P("PS3a1")
    x = p.dram_in("x", (T, D))
    n1w = p.dram_in("n1w", (D,))
    ipw = p.dram_in("ipw", (DI2, D))
    gpw = p.dram_in("gpw", (DI2, D))
    ident_in = p.dram_in("ident", (128, 128), BF16)
    zTo = p.dram_out("zT", (DI2, T), BF16)

    ident = p.sb((128, 128), BF16)
    p.dma(ident, ident_in)
    zT = [p.sb((128, T), BF16) for _ in range(16)]
    psA = p.ps((128, 512))
    psT = p.ps((128, 128), BF16)
    inT = [p.sb((128, DI2), BF16) for _ in range(8)]
    gaT = [p.sb((128, DI2), BF16) for _ in range(8)]
    wstage = p.sb((128, DI2))
    wstb = p.sb((128, D), BF16)
    for (wd, wt) in ((ipw, inT), (gpw, gaT)):
        for rt in range(16):
            p.dma(wstage[:, 0:D], wd[rt * 128:(rt + 1) * 128, :])
            p.vcopy(wstb, wstage[:, 0:D])
            for k in range(8):
                p.trans(psT, wstb[:, k * 128:(k + 1) * 128], ident)
                p.vcopy(wt[k][:, rt * 128:(rt + 1) * 128], psT)
    n1T = p.sb((128, D))
    p.dma(n1T, n1w[None, :].broadcast_to([128, D]))
    xs = p.sb((128, D))
    xn = p.sb((128, D))
    scr = p.sb((128, D))
    stat = p.sb((128, 8))
    xnb = p.sb((128, D), BF16)
    hT_m = [p.sb((128, 128), BF16) for _ in range(8)]
    zf = p.sb((128, DI2))
    gf = p.sb((128, DI2))
    zb = p.sb((128, DI2), BF16)
    for m in range(MT):
        r = slice(m * 128, (m + 1) * 128)
        p.dma(xs, x[r, :])
        emit_ln(p, xs, xn, scr, stat, D)
        emit_rms(p, xn, scr, xs, stat, D, 1e-6)
        p.tt(scr, scr, n1T, ALU.mult)
        _emit_castT(p, scr, xnb, ident, psT, hT_m, 0, 8)
        for n in range(4):
            ns = slice(n * 512, (n + 1) * 512)
            for k in range(8):
                p.mm(psA, hT_m[k], inT[k][:, ns], start=(k == 0), stop=(k == 7))
            p.vcopy(zf[:, ns], psA)
            for k in range(8):
                p.mm(psA, hT_m[k], gaT[k][:, ns], start=(k == 0), stop=(k == 7))
            p.act(gf[:, ns], psA, AF.Sigmoid)
        p.tt(zf, zf, gf, ALU.mult)
        p.vcopy(zb, zf)
        for j in range(16):
            p.trans(psT, zb[:, j * 128:(j + 1) * 128], ident)
            p.vcopy(zT[j][:, m * 128:(m + 1) * 128], psT)
    for k in range(16):
        p.dma(zTo[k * 128:(k + 1) * 128, :], zT[k])
    return p.build()


def build_PS3a2():
    DI2 = 2048
    p = P("PS3a2")
    zT_i = p.dram_in("zT", (DI2, T), BF16)
    dtw = p.dram_in("dtw", (DI2, DI2))
    dtb = p.dram_in("dtb", (DI2,))
    xpw = p.dram_in("xpw", (32, DI2))
    alog = p.dram_in("alog", (DI2, 16))
    dp = p.dram_in("dp", (DI2,))
    ident_in = p.dram_in("ident", (128, 128), BF16)
    yTo = p.dram_out("yT", (DI2, T), BF16)
    dt_d = p.dram_tmp("dt_d", (DI2, T))

    ident = p.sb((128, 128), BF16)
    p.dma(ident, ident_in)
    zT = [p.sb((128, T), BF16) for _ in range(16)]
    for k in range(16):
        p.dma(zT[k], zT_i[k * 128:(k + 1) * 128, :])
    psA = p.ps((128, 512))
    psT = p.ps((128, 128), BF16)

    wstage = p.sb((128, DI2))
    wstb = p.sb((128, DI2), BF16)
    dtwT = [p.sb((128, 128), BF16) for _ in range(16)]
    dtb_c = p.sb((128, 1))
    dtv = p.sb((128, T))
    sp1 = p.sb((128, T))
    sp2 = p.sb((128, T))
    for c in range(16):
        p.dma(wstage, dtw[c * 128:(c + 1) * 128, :])
        p.vcopy(wstb, wstage)
        for k in range(16):
            p.trans(psT, wstb[:, k * 128:(k + 1) * 128], ident)
            p.vcopy(dtwT[k], psT)
        p.dma(dtb_c, dtb[c * 128:(c + 1) * 128].rearrange("(p o) -> p o", o=1))
        for n in range(4):
            ns = slice(n * 512, (n + 1) * 512)
            for k in range(16):
                p.mm(psA, dtwT[k], zT[k][:, ns], start=(k == 0), stop=(k == 15))
            p.ts(dtv[:, ns], psA, dtb_c, ALU.add)
        softplus_emit(p, dtv, dtv, sp1, sp2)
        p.dma(dt_d[c * 128:(c + 1) * 128, :], dtv)

    xpb = p.sb((32, DI2), BF16)
    p.vcopy(xpb[0:32, 0:DI2], wstage[0:32, 0:DI2]) if False else None
    p.dma(wstage[0:32, :], xpw)
    p.vcopy(xpb, wstage[0:32, :])
    xpT = [p.sb((128, 32), BF16) for _ in range(16)]
    for k in range(16):
        p.trans(psT[:, 0:32], xpb[:, k * 128:(k + 1) * 128], ident[0:32, 0:32])
        p.vcopy(xpT[k], psT[:, 0:32])
    BC = p.sb((32, T))
    BCb = p.sb((32, T), BF16)
    ps32 = p.ps((32, 512))
    for n in range(4):
        ns = slice(n * 512, (n + 1) * 512)
        for k in range(16):
            p.mm(ps32, xpT[k], zT[k][:, ns], start=(k == 0), stop=(k == 15))
        p.vcopy(BC[:, ns], ps32)
    p.vcopy(BCb, BC)
    onesb = p.sb((1, 128), BF16)
    p.memset(onesb, 1.0)
    rowB = p.sb((1, T), BF16)
    rowC = p.sb((1, T), BF16)
    psB = p.ps((128, T))
    Bb = sp1
    Cb = sp2
    dt_c = dtv
    dtz = p.sb((128, T))
    zfc = p.sb((128, T))
    hsc = p.sb((128, T))
    yacc = p.sb((128, T))
    hC = p.sb((128, T))
    negA = p.sb((128, 16))
    alf = p.sb((128, 16))
    dp_c = p.sb((128, 1))
    for c in range(16):
        cc = slice(c * 128, (c + 1) * 128)
        p.dma(dt_c, dt_d[cc, :])
        p.dma(alf, alog[cc, :])
        p.act(negA, alf, AF.Exp)
        p.ts(negA, negA, -1.0, ALU.mult)
        p.dma(dp_c, dp[c * 128:(c + 1) * 128].rearrange("(p o) -> p o", o=1))
        p.vcopy(zfc, zT[c])
        p.tt(dtz, dt_c, zfc, ALU.mult)
        p.memset(yacc, 0.0)
        for n in range(16):
            p.dma(rowB, BCb[n:n + 1, :])
            p.dma(rowC, BCb[16 + n:17 + n, :])
            for q4 in range(4):
                ns = slice(q4 * 512, (q4 + 1) * 512)
                p.mm(psB[:, ns], onesb, rowB[0:1, ns], start=True, stop=True)
            p.vcopy(Bb, psB)
            for q4 in range(4):
                ns = slice(q4 * 512, (q4 + 1) * 512)
                p.mm(psB[:, ns], onesb, rowC[0:1, ns], start=True, stop=True)
            p.vcopy(Cb, psB)
            p.act(hC, dt_c, AF.Exp, scale=negA[:, n:n + 1])
            p.tt(Bb, Bb, dtz, ALU.mult)
            for s in range(4):
                ss = slice(s * 512, (s + 1) * 512)
                init = 0.0 if s == 0 else hsc[:, s * 512 - 1:s * 512]
                p.scan(hsc[:, ss], hC[:, ss], Bb[:, ss], init)
            p.tt(hC, hsc, Cb, ALU.mult)
            p.tt(yacc, yacc, hC, ALU.add)
        p.stt(hC, zfc, dp_c, yacc, ALU.mult, ALU.add)
        p.vcopy(zT[c], hC)
        p.dma(yTo[cc, :], zT[c])
    return p.build()


def build_PS3b():
    DI2 = 2048
    p = P("PS3b")
    x = p.dram_in("x", (T, D))
    yT_i = p.dram_in("yT", (DI2, T), BF16)
    opw = p.dram_in("opw", (D, DI2))
    ident_in = p.dram_in("ident", (128, 128), BF16)
    x2_o = p.dram_out("x2", (T, D))

    ident = p.sb((128, 128), BF16)
    p.dma(ident, ident_in)
    yT = [p.sb((128, T), BF16) for _ in range(16)]
    for k in range(16):
        p.dma(yT[k], yT_i[k * 128:(k + 1) * 128, :])
    opT = [p.sb((128, D), BF16) for _ in range(16)]
    wstage = p.sb((128, DI2))
    wstb = p.sb((128, DI2), BF16)
    psT = p.ps((128, 128), BF16)
    for rt in range(8):
        p.dma(wstage, opw[rt * 128:(rt + 1) * 128, :])
        p.vcopy(wstb, wstage)
        for k in range(16):
            p.trans(psT, wstb[:, k * 128:(k + 1) * 128], ident)
            p.vcopy(opT[k][:, rt * 128:(rt + 1) * 128], psT)
    xs = p.sb((128, D))
    xn = p.sb((128, D))
    scr = p.sb((128, D))
    stat = p.sb((128, 8))
    psA = p.ps((128, 512))
    for m in range(MT):
        r = slice(m * 128, (m + 1) * 128)
        p.dma(xs, x[r, :])
        emit_ln(p, xs, xn, scr, stat, D)
        for n in range(2):
            ns = slice(n * 512, (n + 1) * 512)
            for k in range(16):
                p.mm(psA, yT[k][:, r], opT[k][:, ns], start=(k == 0), stop=(k == 15))
            p.tt(xn[:, ns], xn[:, ns], psA, ALU.add)
        p.dma(x2_o[r, :], xn)
    return p.build()


def build_PS3c():
    FH = 2752
    p = P("PS3c")
    x = p.dram_in("x", (T, D))
    x2 = p.dram_in("x2", (T, D))
    fnw = p.dram_in("fnw", (D,))
    gw = p.dram_in("gw", (FH, D))
    uw = p.dram_in("uw", (FH, D))
    dw = p.dram_in("dw", (D, FH))
    ident_in = p.dram_in("ident", (128, 128), BF16)
    xo = p.dram_out("xo", (T, D))

    ident = p.sb((128, 128), BF16)
    p.dma(ident, ident_in)
    kt3 = [(k * 128, min(128, FH - k * 128)) for k in range((FH + 127) // 128)]
    gT = [p.sb((128, FH), BF16) for _ in range(8)]
    uT = [p.sb((128, FH), BF16) for _ in range(8)]
    dT = [p.sb((128, D), BF16) for _ in range(len(kt3))]
    wstage = p.sb((128, FH))
    wstb = p.sb((128, D), BF16)
    psT = p.ps((128, 128), BF16)
    for (wd, wt) in ((gw, gT), (uw, uT)):
        for (r0, rows) in kt3:
            p.dma(wstage[0:rows, 0:D], wd[r0:r0 + rows, :])
            p.vcopy(wstb[0:rows, :], wstage[0:rows, 0:D])
            for k in range(8):
                p.trans(psT[0:128, 0:rows], wstb[0:rows, k * 128:(k + 1) * 128],
                        ident[0:rows, 0:rows])
                p.vcopy(wt[k][:, r0:r0 + rows], psT[0:128, 0:rows])
    wstb2 = p.sb((128, FH), BF16)
    for rt in range(8):
        p.dma(wstage, dw[rt * 128:(rt + 1) * 128, :])
        p.vcopy(wstb2, wstage)
        for ki, (r0, rows) in enumerate(kt3):
            p.trans(psT[0:rows, :], wstb2[:, r0:r0 + rows], ident)
            p.vcopy(dT[ki][0:rows, rt * 128:(rt + 1) * 128], psT[0:rows, :])
    fT = p.sb((128, D))
    p.dma(fT, fnw[None, :].broadcast_to([128, D]))
    xs = p.sb((128, D))
    x2s = p.sb((128, D))
    hn = p.sb((128, D))
    scr = p.sb((128, D))
    stat = p.sb((128, 8))
    hnb = p.sb((128, D), BF16)
    hT_m = [p.sb((128, 128), BF16) for _ in range(8)]
    ff = wstage
    gch = p.sb((128, 512))
    ffb = p.sb((128, FH), BF16)
    ffT = [p.sb((128, 128), BF16) for _ in range(len(kt3))]
    psA = p.ps((128, 512))
    psG = p.ps((128, 512))
    chunks = [(c * 512, min(512, FH - c * 512)) for c in range((FH + 511) // 512)]
    for m in range(MT):
        r = slice(m * 128, (m + 1) * 128)
        p.dma(x2s, x2[r, :])
        emit_rms(p, x2s, hn, scr, stat, D, 1e-6)
        p.tt(hn, hn, fT, ALU.mult)
        _emit_castT(p, hn, hnb, ident, psT, hT_m, 0, 8)
        for (c0, Na) in chunks:
            for k in range(8):
                p.mm(psG[:, 0:Na], hT_m[k], gT[k][:, c0:c0 + Na],
                     start=(k == 0), stop=(k == 7))
            for k in range(8):
                p.mm(psA[:, 0:Na], hT_m[k], uT[k][:, c0:c0 + Na],
                     start=(k == 0), stop=(k == 7))
            p.act(gch[:, 0:Na], psG[:, 0:Na], AF.Silu)
            p.tt(ff[:, c0:c0 + Na], psA[:, 0:Na], gch[:, 0:Na], ALU.mult)
        p.vcopy(ffb, ff)
        for ki, (r0, rows) in enumerate(kt3):
            p.trans(psT[0:rows, :], ffb[:, r0:r0 + rows], ident)
            p.vcopy(ffT[ki][0:rows, :], psT[0:rows, :])
        p.dma(xs, x[r, :])
        p.tt(x2s, x2s, xs, ALU.add)
        for n in range(2):
            ns = slice(n * 512, (n + 1) * 512)
            for ki, (r0, rows) in enumerate(kt3):
                p.mm(psA, ffT[ki][0:rows, :], dT[ki][0:rows, ns],
                     start=(ki == 0), stop=(ki == len(kt3) - 1))
            p.tt(x2s[:, ns], x2s[:, ns], psA, ALU.add)
        p.dma(xo[r, :], x2s)
    return p.build()


def build_PQ3():
    p = P("PQ3")
    x = p.dram_in("x", (T, D))
    qw = p.dram_in("qw", (6, D, D), I32)
    sc = p.dram_in("sc", (6, D, 64))
    bi = p.dram_in("bi", (6, D))
    la = p.dram_in("la", (6, 32, D))
    lb = p.dram_in("lb", (6, D, 32))
    lng = p.dram_in("lng", (3, D))
    lnb = p.dram_in("lnb", (3, D))
    ident_in = p.dram_in("ident", (128, 128), BF16)
    xo = p.dram_out("xo", (T, D))

    ident = p.sb((128, 128), BF16)
    p.dma(ident, ident_in)
    h = [p.sb((128, D)) for _ in range(MT)]
    xs = p.sb((128, D))
    scr = p.sb((128, D))
    y1 = p.sb((128, D))
    stat = p.sb((128, 8))
    hb = p.sb((128, D), BF16)
    hT_m = [p.sb((128, 128), BF16) for _ in range(8)]
    psA = p.ps((128, 512))
    ps32 = p.ps((128, 32))
    psT = p.ps((128, 128), BF16)

    for m in range(MT):
        p.dma(xs, x[m * 128:(m + 1) * 128, :])
        emit_ln(p, xs, h[m], scr, stat, D)

    wT = [[p.sb((128, D), BF16) for _ in range(8)] for _ in range(2)]
    laT = [[p.sb((128, 32), BF16) for _ in range(8)] for _ in range(2)]
    lbT = [p.sb((32, D), BF16) for _ in range(2)]
    biT = [p.sb((128, D)) for _ in range(2)]
    gT_ = p.sb((128, D))
    bT_ = p.sb((128, D))
    qstage = p.sb((128, D), I32)
    fstage = p.sb((128, D))
    scst = p.sb((128, 64))
    stb = p.sb((128, D), BF16)
    la_s = p.sb((32, D))
    la_b = p.sb((32, D), BF16)
    rb = p.sb((128, 32), BF16)
    rf = p.sb((128, 32))

    def load_linear(i, slot):
        for ot in range(8):
            rows = slice(ot * 128, (ot + 1) * 128)
            p.dma(qstage, qw[i][rows, :])
            p.dma(scst, sc[i][rows, :])
            p.vcopy(fstage, qstage)
            p.ts2(fstage, fstage, 1.0 / 7.5, ALU.mult, -1.0, ALU.add)
            fv = fstage.rearrange("p (g r) -> p g r", r=16)
            p.tt(fv, fv, scst[:, :, None].broadcast_to([128, 64, 16]), ALU.mult)
            p.vcopy(stb, fstage)
            for k in range(8):
                p.trans(psT, stb[:, k * 128:(k + 1) * 128], ident)
                p.vcopy(wT[slot][k][:, ot * 128:(ot + 1) * 128], psT)
        p.dma(la_s, la[i])
        p.vcopy(la_b, la_s)
        for k in range(8):
            p.trans(psT[:, 0:32], la_b[:, k * 128:(k + 1) * 128], ident[0:32, 0:32])
            p.vcopy(laT[slot][k], psT[:, 0:32])
        for k in range(8):
            p.dma(fstage[:, 0:32], lb[i][k * 128:(k + 1) * 128, :])
            p.vcopy(rb, fstage[:, 0:32])
            p.trans(psT[0:32, :], rb, ident)
            p.vcopy(lbT[slot][:, k * 128:(k + 1) * 128], psT[0:32, :])
        p.dma(biT[slot], bi[i][None, :].broadcast_to([128, D]))

    def qlora(src_m, slot, out):
        for k in range(8):
            p.mm(ps32, hT_m[k], laT[slot][k], start=(k == 0), stop=(k == 7))
        p.vcopy(rf, ps32)
        p.vcopy(rb, rf)
        p.trans(psT[0:32, :], rb, ident)
        p.vcopy(stb[0:32, 0:128], psT[0:32, :])
        for n in range(2):
            ns = slice(n * 512, (n + 1) * 512)
            for k in range(8):
                p.mm(psA, hT_m[k], wT[slot][k][:, ns], start=(k == 0), stop=False)
            p.mm(psA, stb[0:32, 0:128], lbT[slot][0:32, ns], start=False, stop=True)
            p.tt(out[:, ns], psA, biT[slot][:, ns], ALU.add)

    for blk in range(3):
        load_linear(2 * blk, 0)
        load_linear(2 * blk + 1, 1)
        p.dma(gT_, lng[blk][None, :].broadcast_to([128, D]))
        p.dma(bT_, lnb[blk][None, :].broadcast_to([128, D]))
        for m in range(MT):
            _emit_castT(p, h[m], hb, ident, psT, hT_m, 0, 8)
            qlora(m, 0, y1)
            emit_ln(p, y1, scr, xs, stat, D)
            p.tt(scr, scr, gT_, ALU.mult)
            p.tt(scr, scr, bT_, ALU.add)
            p.act(y1, scr, AF.Gelu)
            _emit_castT(p, y1, hb, ident, psT, hT_m, 0, 8)
            qlora(m, 1, scr)
            p.tt(h[m], h[m], scr, ALU.add)
    for m in range(MT):
        p.dma(xs, x[m * 128:(m + 1) * 128, :])
        p.stt(scr, h[m], 0.5, xs, ALU.mult, ALU.add)
        p.dma(xo[m * 128:(m + 1) * 128, :], scr)
    return p.build()


_PROGS = {}


def _prog(name):
    if name not in _PROGS:
        _PROGS[name] = globals()[f"build_{name}"]()
    return _PROGS[name]


def _launch(name, im_list, outs):
    nc = _prog(name)
    res = run_bass_kernel_spmd(nc, im_list, core_ids=list(range(NCORE)))
    return [[np.asarray(res.results[c][o]) for o in outs] for c in range(NCORE)]


_IDENT = np.eye(128, dtype=np.float32).astype(BF)
TRACE = None


def _tr(name, xs):
    if TRACE is not None:
        TRACE.append((name, xs[0].copy()))


def kernel(**inp):
    f32 = np.float32
    xs = [np.ascontiguousarray(inp["L0_attn_x"][c], dtype=f32) for c in range(NCORE)]
    ident = _IDENT
    csq, ssq, csk, ssk = _rope_tables_inner()
    mCn, mBn, mAn = _masks(strict=False)
    cos0, sin0 = _rope_tables_l0attn()
    mCs, mBs, mAs = _masks(strict=True)

    def bcast(d):
        return [d] * NCORE

    def run1(name, per_core_dicts, outs=("xo",)):
        return _launch(name, per_core_dicts, list(outs))

    def l0_ffn_chain():
        nonlocal xs
        z = [r[0] for r in run1("PL", [dict(x=xs[c]) for c in range(NCORE)], ("z",))]
        for i in range(6):
            wq = np.ascontiguousarray(inp["L0_ffn_Wqkv"][i], f32)
            wo = np.ascontiguousarray(inp["L0_ffn_Wo"][i], f32)
            qs = np.ascontiguousarray(inp["L0_ffn_q_scale"][i], f32)
            ks = np.ascontiguousarray(inp["L0_ffn_k_scale"][i], f32)
            z = [r[0] for r in run1("PA", [dict(
                z=z[c], Wqkv=wq, Wo=wo, qs=qs, ks=ks, csq=csq, ssq=ssq, csk=csk,
                ssk=ssk, mC=mCn, mB=mBn, mA=mAn, ident=ident)
                for c in range(NCORE)], ("zo",))]
            w1 = np.ascontiguousarray(inp["L0_ffn_W1"][i], f32)
            b1 = np.ascontiguousarray(inp["L0_ffn_b1"][i], f32)
            w2 = np.ascontiguousarray(inp["L0_ffn_W2"][i], f32)
            b2 = np.ascontiguousarray(inp["L0_ffn_b2"][i], f32)
            g = np.ascontiguousarray(inp["L0_ffn_ln_g"][i], f32)
            bl = np.ascontiguousarray(inp["L0_ffn_ln_b"][i], f32)
            z = [r[0] for r in run1("PF", [dict(
                z=z[c], W1=w1, b1=b1, W2=w2, b2=b2, g=g, bl=bl, ident=ident)
                for c in range(NCORE)], ("zo",))]
        xs = [r[0] for r in run1("PX", [dict(x=xs[c], z=z[c])
                                        for c in range(NCORE)], ("xo",))]

    def l2_moe():
        nonlocal xs
        res = run1("PM2a", [dict(
            x=xs[c], rw=inp["L2_ffn_router_w"], eb=inp["L2_ffn_experts_bias"],
            sw1=np.ascontiguousarray(inp["L2_ffn_sw1"][0], f32),
            sw3=np.ascontiguousarray(inp["L2_ffn_sw3"][0], f32),
            sw2=np.ascontiguousarray(inp["L2_ffn_sw2"][0], f32), ident=ident)
            for c in range(NCORE)], ("acc", "xn", "cw"))
        acc = [r[0] for r in res]
        xn = [r[1] for r in res]
        cw = [r[2] for r in res]
        for e in range(8):
            acc = [r[0] for r in run1("PM2b", [dict(
                xn=xn[c], cw1=np.ascontiguousarray(cw[c][:, e:e + 1]),
                acc_in=acc[c],
                w1p=np.ascontiguousarray(inp["L2_ffn_w1"][e], f32),
                w3p=np.ascontiguousarray(inp["L2_ffn_w3"][e], f32),
                w2p=np.ascontiguousarray(inp["L2_ffn_w2"][e], f32), ident=ident)
                for c in range(NCORE)], ("acc",))]
        xs = acc

    l0_ffn_chain()
    _tr("ffn0a", xs)
    xs = [r[0] for r in run1("PA0", [dict(
        x=xs[c], Wqkv=inp["L0_attn_Wqkv"], Wout=inp["L0_attn_Wout"],
        cos=cos0, sin=sin0, mC=mCs, mB=mBs, mA=mAs, ident=ident)
        for c in range(NCORE)])]
    _tr("a0", xs)
    l0_ffn_chain()
    _tr("b0", xs)

    def pq1():
        nonlocal xs
        xs = [r[0] for r in run1("PQ1", [dict(
            x=xs[c], gamma=inp["L1_ffn_ln_gamma"], beta=inp["L1_ffn_ln_beta"],
            q1=inp["L1_ffn_fc1_q"], s1=inp["L1_ffn_fc1_scales"],
            bb1=inp["L1_ffn_fc1_bias"], q2=inp["L1_ffn_fc2_q"],
            s2=inp["L1_ffn_fc2_scales"], bb2=inp["L1_ffn_fc2_bias"], ident=ident)
            for c in range(NCORE)])]
    pq1()
    _tr("q1a", xs)
    xs = [r[0] for r in run1("PS1", [dict(
        x=xs[c], rmsw=inp["L1_attn_rms_weight"], wA=inp["L1_attn_w_A"],
        bA=inp["L1_attn_b_A"], wx=inp["L1_attn_w_x"],
        wsg=inp["L1_attn_w_sidegate"], wout=inp["L1_attn_w_output"],
        wo=inp["L1_attn_w_o"], ident=ident) for c in range(NCORE)])]
    _tr("s1", xs)
    pq1()
    _tr("b1", xs)

    l2_moe()
    _tr("m2a", xs)
    qkv = run1("PA2q", [dict(x=xs[c], Wqkv=inp["L2_attn_W_qkv"],
                             bqkv=inp["L2_attn_b_qkv"], ident=ident)
                        for c in range(NCORE)], ("qT", "kT", "v"))
    oT = []
    for half in range(2):
        rs = slice(half * 512, (half + 1) * 512)
        oT.append([r[0] for r in run1("PA2h", [dict(
            qTh=np.ascontiguousarray(qkv[c][0][rs]),
            kTh=np.ascontiguousarray(qkv[c][1][rs]),
            vh=np.ascontiguousarray(qkv[c][2][:, rs]), ident=ident)
            for c in range(NCORE)], ("oTh",))])
    xs = [r[0] for r in run1("PA2o", [dict(
        x=xs[c], oT0=oT[0][c], oT1=oT[1][c], Wout=inp["L2_attn_W_out"],
        bout=inp["L2_attn_b_out"], ident=ident) for c in range(NCORE)])]
    _tr("a2", xs)
    l2_moe()
    _tr("b2", xs)

    def pq3():
        nonlocal xs
        xs = [r[0] for r in run1("PQ3", [dict(
            x=xs[c], qw=inp["L3_ffn_qweights"], sc=inp["L3_ffn_scales"],
            bi=inp["L3_ffn_biases"], la=inp["L3_ffn_lora_a"],
            lb=inp["L3_ffn_lora_b"], lng=inp["L3_ffn_ln_gamma"],
            lnb=inp["L3_ffn_ln_beta"], ident=ident) for c in range(NCORE)])]
    pq3()
    _tr("q3", xs)
    zT = [r[0] for r in run1("PS3a1", [dict(
        x=xs[c], n1w=inp["L3_attn_norm1_w"], ipw=inp["L3_attn_in_proj_w"],
        gpw=inp["L3_attn_gate_proj_w"], ident=ident)
        for c in range(NCORE)], ("zT",))]
    yT = [r[0] for r in run1("PS3a2", [dict(
        zT=zT[c], dtw=inp["L3_attn_dt_w"], dtb=inp["L3_attn_dt_b"],
        xpw=inp["L3_attn_x_proj_w"], alog=inp["L3_attn_A_log"],
        dp=inp["L3_attn_Dp"], ident=ident) for c in range(NCORE)], ("yT",))]
    x2 = [r[0] for r in run1("PS3b", [dict(
        x=xs[c], yT=yT[c], opw=inp["L3_attn_out_proj_w"], ident=ident)
        for c in range(NCORE)], ("x2",))]
    xs = [r[0] for r in run1("PS3c", [dict(
        x=xs[c], x2=x2[c], fnw=inp["L3_attn_ffn_norm_w"],
        gw=inp["L3_attn_ffn_gate_w"], uw=inp["L3_attn_ffn_up_w"],
        dw=inp["L3_attn_ffn_down_w"], ident=ident) for c in range(NCORE)])]
    _tr("s3", xs)
    pq3()

    return np.stack(xs, axis=0)
